# version 02 (final)
# baseline (speedup 1.0000x reference)
import numpy as np
import ml_dtypes
import concourse.bass as bass
import concourse.mybir as mybir
from concourse.bass_utils import run_bass_kernel_spmd
from concourse.aot_env import allow_ici_all_to_all

F32 = mybir.dt.float32
BF16 = mybir.dt.bfloat16
AF = mybir.ActivationFunctionType
ALU = mybir.AluOpType
AX = mybir.AxisListType

NCORES = 8
D = 2048
T = 8192
TOK = T // NCORES
R = 512
HID = 8192
NE = 2
NCH = 16
NQC = 16
NGRP = 4
GH = 16
MASK_NEG = -30000.0
PEND_BUDGET = 1536
RSCALE = float(1.0 / np.sqrt(128.0))


class Pacer:
    def __init__(self, sem):
        self.sem = sem
        self.pref = [0]
        self.last_wait = 0

    def mm(self, tt, out, lhsT, rhs, start, stop, ncols):
        if start and stop:
            tt.matmul(out, lhsT, rhs, start=True, stop=True, skip_group_check=True)
            return
        need = self.pref[-1] + ncols - PEND_BUDGET
        if need > 0:
            lo, hi = 0, len(self.pref) - 1
            while lo < hi:
                mid = (lo + hi) // 2
                if self.pref[mid] >= need:
                    hi = mid
                else:
                    lo = mid + 1
            if lo > self.last_wait:
                tt.wait_ge(self.sem, lo)
                self.last_wait = lo
        tt.matmul(out, lhsT, rhs, start=start, stop=stop,
                  skip_group_check=True).then_inc(self.sem, 1)
        self.pref.append(self.pref[-1] + ncols)

    @property
    def count(self):
        return len(self.pref) - 1


class Ctr:
    def __init__(self, sem):
        self.sem = sem
        self.v = 0

    def inc(self, ins, by=1):
        ins.then_inc(self.sem, by)
        self.v += by
        return self.v


def build_nc(upto='full'):
    nc = bass.Bass("TRN2", target_bir_lowering=False)

    d_x = nc.dram_tensor("x_fm", [128, NCH, TOK], F32, kind="ExternalInput").ap()
    d_fc = nc.dram_tensor("fc_s", [NE, NGRP, GH, 16, 128, 128], F32, kind="ExternalInput").ap()
    d_pj = nc.dram_tensor("pj_s", [NE, NGRP, 16, GH, 128, 128], F32, kind="ExternalInput").ap()
    d_wr = nc.dram_tensor("wr", [128, NCH, NE], F32, kind="ExternalInput").ap()
    d_wq = nc.dram_tensor("wqdT", [128, NCH, R], F32, kind="ExternalInput").ap()
    d_wk = nc.dram_tensor("wkdT", [128, NCH, R], F32, kind="ExternalInput").ap()
    d_wv = nc.dram_tensor("wvdT", [128, NCH, R], F32, kind="ExternalInput").ap()
    d_qup = nc.dram_tensor("qup", [128, 2, R], F32, kind="ExternalInput").ap()
    d_kup = nc.dram_tensor("kup", [128, 2, R], F32, kind="ExternalInput").ap()
    d_vupT = nc.dram_tensor("vupT", [128, 2, 4, 128], F32, kind="ExternalInput").ap()
    d_wo = nc.dram_tensor("woT_s", [16, 16, 128, 128], F32, kind="ExternalInput").ap()
    d_ob = nc.dram_tensor("outb", [128, NCH], F32, kind="ExternalInput").ap()
    d_mask = nc.dram_tensor("mask", [128, 4, 512], F32, kind="ExternalInput").ap()
    d_id = nc.dram_tensor("ident", [128, 128], BF16, kind="ExternalInput").ap()
    d_out = nc.dram_tensor("y", [128, NCH, TOK], F32, kind="ExternalOutput").ap()

    d_qkv_in = nc.dram_tensor("cc_qkv_in", [3 * R, TOK], BF16, kind="Internal").ap()
    d_qkv_out = nc.dram_tensor("cc_qkv_out", [NCORES * 3 * R, TOK], BF16, kind="Internal").ap()
    d_vup = nc.dram_tensor("vup_scr", [2, 64, 128, 128], BF16, kind="Internal").ap()
    d_ctx_in = nc.dram_tensor("cc_ctx_in", [D, TOK], BF16, kind="Internal").ap()
    d_ctx_out = nc.dram_tensor("cc_ctx_out", [D, TOK], BF16, kind="Internal").ap()
    d_x2 = nc.dram_tensor("x2_scr", [128, NCH, TOK], BF16, kind="Internal").ap()
    rg = [list(range(NCORES))]

    from contextlib import ExitStack
    es = ExitStack()
    _uid = [0]
    def uniq(n):
        _uid[0] += 1
        return "%s_u%d" % (n, _uid[0])
    def S(*a):
        return es.enter_context(nc.sbuf_tensor(*a))
    def SEM(n):
        return es.enter_context(nc.semaphore(n))
    if True:
        id_sb = S("id_sb", [128, 128], BF16)
        ones_col = S("ones_col", [128, 1], BF16)
        ones_row = S("ones_row", [1, 128], BF16)
        wr_sb = S("wr_sb", [128, NCH, NE], BF16)
        ob_sb = S("ob_sb", [128, NCH], F32)
        qup_sb = S("qup_sb", [128, 2, R], BF16)
        kup_sb = S("kup_sb", [128, 2, R], BF16)
        vupT_sb = S("vupT_sb", [128, 2, 4, 128], BF16)
        gbc = S("gbc", [128, NE, TOK], BF16)
        colstat = S("colstat", [128, 16], F32)
        cst = S("cst", [128, 1], F32)
        B1 = S("B1", [128, NCH, TOK], BF16)
        B2 = S("B2", [128, NCH, TOK], BF16)
        B3 = S("B3", [128, NCH, TOK], BF16)
        F1a = S("F1a", [128, 8, TOK], F32)
        F1b = S("F1b", [128, 8, TOK], F32)
        sem_pace = SEM("pace")
        sem_gl = SEM("s_gl")
        sem_sl = SEM("s_sl")
        sem_pe = SEM("s_pe")
        sem_sc = SEM("s_sc")
        sem_dv = SEM("s_dv")
        sem_cc = SEM("s_cc")
        sem_out = SEM("s_out")
        block = es.enter_context(nc.Block())
        pc = Pacer(sem_pace)
        gl, sl, pe, sc, dv, cc, so = (Ctr(s) for s in (
            sem_gl, sem_sl, sem_pe, sem_sc, sem_dv, sem_cc, sem_out))

        F1bf = F1b.ap().rearrange("p a b -> p (a b)")
        F1af = F1a.ap().rearrange("p a b -> p (a b)")
        B1f = B1.ap().rearrange("p a b -> p (a b)")
        B2f = B2.ap().rearrange("p a b -> p (a b)")
        B3f = B3.ap().rearrange("p a b -> p (a b)")

        def row(slot, lo=0, n=1024):
            return F1bf[0:1, slot * 1024 + lo: slot * 1024 + lo + n]

        def fence():
            vals = (pc.count, sc.v, dv.v, pe.v, gl.v, sl.v)
            for eng in (block.tensor, block.scalar, block.vector,
                        block.gpsimd, block.sync):
                @eng
                def _(x, vals=vals):
                    x.wait_ge(sem_pace, vals[0])
                    x.wait_ge(sem_sc, vals[1])
                    x.wait_ge(sem_dv, vals[2])
                    x.wait_ge(sem_pe, vals[3])
                    x.wait_ge(sem_gl, vals[4])
                    x.wait_ge(sem_sl, vals[5])


        def dump_and_stop(src_sb, rdy_sem, rdy_val):
            @block.gpsimd
            def _(g):
                g.wait_ge(rdy_sem, rdy_val)
                so.inc(g.dma_start(d_out[:], src_sb[:]), 16)
            @block.sync
            def _(sy):
                sy.wait_ge(sem_out, so.v)

        @block.gpsimd
        def _(g):
            gl.inc(g.dma_start(B1[:], d_x), 16)
            gl.inc(g.dma_start(id_sb[:], d_id), 16)
            gl.inc(g.dma_start(wr_sb[:], d_wr), 16)
            gl.inc(g.dma_start(ob_sb[:], d_ob), 16)
            gl.inc(g.dma_start(qup_sb[:], d_qup), 16)
            gl.inc(g.dma_start(kup_sb[:], d_kup), 16)
            gl.inc(g.dma_start(vupT_sb[:], d_vupT), 16)
        GL_P0 = gl.v

        @block.vector
        def _(v):
            v.memset(ones_col[:], 1.0)
            v.memset(ones_row[:], 1.0)
            v.memset(cst[:], 1e-5)
            dv.inc(v.engine_nop())
        DV_ONES = dv.v

        def emit_layernorm(x_src, rdy_sem, rdy_val):
            with (
                nc.psum_tensor(uniq("lps0"), [1, 512], F32) as lps0,
                nc.psum_tensor(uniq("lps1"), [1, 512], F32) as lps1,
                nc.psum_tensor(uniq("lps2"), [1, 512], F32) as lps2,
                nc.psum_tensor(uniq("lps3"), [1, 512], F32) as lps3,
                nc.psum_tensor(uniq("lpb0"), [128, 512], F32) as lpb0,
                nc.psum_tensor(uniq("lpb1"), [128, 512], F32) as lpb1,
                nc.sbuf_tensor(uniq("xsq"), [128, 2, 1024], BF16) as xsq,
                nc.sbuf_tensor(uniq("rowbf"), [1, 2048], BF16) as rowbf,
                nc.sbuf_tensor(uniq("bc_mb"), [128, 1024], F32) as bc_mb,
                nc.sbuf_tensor(uniq("bc_rb"), [128, 1024], F32) as bc_rb,
            ):
                sq_done = []
                chunk_done = []
                for dc in range(NCH):
                    @block.scalar
                    def _(s, dc=dc):
                        if dc == 0:
                            s.wait_ge(rdy_sem, rdy_val)
                        if dc >= 2:
                            s.wait_ge(sem_pe, chunk_done[dc - 2])
                        sq_done.append(sc.inc(s.activation(
                            xsq[:, dc % 2, :], x_src[:, dc, :], AF.Square)))
                    @block.tensor
                    def _(tt, dc=dc):
                        if dc == 0:
                            tt.wait_ge(rdy_sem, rdy_val)
                            tt.wait_ge(sem_dv, DV_ONES)
                        for th in range(2):
                            pc.mm(tt, (lps0 if th == 0 else lps1)[:], ones_col[:],
                                  x_src[:, dc, th * 512:(th + 1) * 512],
                                  start=(dc == 0), stop=(dc == NCH - 1), ncols=512)
                        tt.wait_ge(sem_sc, sq_done[dc])
                        for th in range(2):
                            pc.mm(tt, (lps2 if th == 0 else lps3)[:], ones_col[:],
                                  xsq[:, dc % 2, th * 512:(th + 1) * 512],
                                  start=(dc == 0), stop=(dc == NCH - 1), ncols=512)
                        chunk_done.append(pe.inc(tt.nop()))
                PC_STATS = pc.count
                @block.scalar
                def _(s):
                    s.wait_ge(sem_pace, PC_STATS)
                    s.copy(row(0, 0, 512), lps0[:])
                    s.copy(row(0, 512, 512), lps1[:])
                    s.copy(row(1, 0, 512), lps2[:])
                    s.copy(row(1, 512, 512), lps3[:])
                    s.mul(row(2), row(0), 1.0 / D)
                    s.mul(row(3), row(1), 1.0 / D)
                    sc.inc(s.activation(row(4), row(2), AF.Square))
                SC_M2 = sc.v
                @block.vector
                def _(v):
                    v.wait_ge(sem_sc, SC_M2)
                    dv.inc(v.tensor_sub(row(5), row(3), row(4)))
                DV_VAR = dv.v
                @block.scalar
                def _(s):
                    s.wait_ge(sem_dv, DV_VAR)
                    s.activation(row(4), row(5), AF.Ln, bias=cst[0:1, 0:1])
                    s.activation(row(5), row(4), AF.Exp, scale=-0.5)
                    s.copy(rowbf[:, 0:1024], row(2))
                    sc.inc(s.copy(rowbf[:, 1024:2048], row(5)))
                SC_ROWS = sc.v
                @block.tensor
                def _(tt):
                    tt.wait_ge(sem_sc, SC_ROWS)
                    for th in range(2):
                        pc.mm(tt, (lpb0 if th == 0 else lpb1)[:], ones_row[:],
                              rowbf[0:1, th * 512:(th + 1) * 512],
                              start=True, stop=True, ncols=512)
                    pe.inc(tt.nop())
                PE_BCm = pe.v
                @block.scalar
                def _(s):
                    s.wait_ge(sem_pe, PE_BCm)
                    s.copy(bc_mb[:, 0:512], lpb0[:])
                    sc.inc(s.copy(bc_mb[:, 512:1024], lpb1[:]))
                SC_BCm = sc.v
                @block.tensor
                def _(tt):
                    tt.wait_ge(sem_sc, SC_BCm)
                    for th in range(2):
                        pc.mm(tt, (lpb0 if th == 0 else lpb1)[:], ones_row[:],
                              rowbf[0:1, 1024 + th * 512:1024 + (th + 1) * 512],
                              start=True, stop=True, ncols=512)
                    pe.inc(tt.nop())
                PE_BCr = pe.v
                @block.scalar
                def _(s):
                    s.wait_ge(sem_pe, PE_BCr)
                    s.copy(bc_rb[:, 0:512], lpb0[:])
                    sc.inc(s.copy(bc_rb[:, 512:1024], lpb1[:]))
                SC_BCr = sc.v
                @block.vector
                def _(v):
                    v.wait_ge(sem_sc, SC_BCr)
                    for dc in range(NCH):
                        v.tensor_sub(F1a[:, 0, :], x_src[:, dc, :], bc_mb[:])
                        v.tensor_mul(B2[:, dc, :], F1a[:, 0, :], bc_rb[:])
                    dv.inc(v.engine_nop())
            return dv.v

        def emit_router(xn_done):
            with (
                nc.psum_tensor(uniq("rp0"), [1, 512], F32) as rp0,
                nc.psum_tensor(uniq("rp1"), [1, 512], F32) as rp1,
                nc.psum_tensor(uniq("rp2"), [1, 512], F32) as rp2,
                nc.psum_tensor(uniq("rp3"), [1, 512], F32) as rp3,
                nc.psum_tensor(uniq("rpb0"), [128, 512], F32) as rpb0,
                nc.psum_tensor(uniq("rpb1"), [128, 512], F32) as rpb1,
                nc.sbuf_tensor(uniq("grow"), [1, 2048], BF16) as grow,
            ):
                pss = [rp0, rp1, rp2, rp3]
                @block.tensor
                def _(tt):
                    tt.wait_ge(sem_dv, xn_done)
                    for e in range(NE):
                        for th in range(2):
                            for dc in range(NCH):
                                pc.mm(tt, pss[e * 2 + th][:], wr_sb[:, dc, e:e + 1],
                                      B2[:, dc, th * 512:(th + 1) * 512],
                                      start=(dc == 0), stop=(dc == NCH - 1), ncols=512)
                PC_R = pc.count
                @block.scalar
                def _(s):
                    s.wait_ge(sem_pace, PC_R)
                    s.copy(row(0, 0, 512), rp0[:])
                    s.copy(row(0, 512, 512), rp1[:])
                    s.copy(row(1, 0, 512), rp2[:])
                    sc.inc(s.copy(row(1, 512, 512), rp3[:]))
                SC_L = sc.v
                @block.vector
                def _(v):
                    v.wait_ge(sem_sc, SC_L)
                    dv.inc(v.tensor_sub(row(2), row(0), row(1)))
                DV_D = dv.v
                @block.scalar
                def _(s):
                    s.wait_ge(sem_dv, DV_D)
                    sc.inc(s.activation(row(3), row(2), AF.Exp, scale=-1.0))
                SC_E = sc.v
                @block.vector
                def _(v):
                    v.wait_ge(sem_sc, SC_E)
                    v.tensor_scalar(row(4), row(3), 1.0, None, ALU.add)
                    v.reciprocal(row(4), row(4))
                    v.tensor_single_scalar(row(5), row(2), 0.0, ALU.is_ge)
                    v.tensor_mul(row(6), row(4), row(5))
                    v.tensor_scalar(row(7), row(4), -1.0, 1.0, ALU.mult, ALU.add)
                    v.tensor_scalar(row(5), row(5), -1.0, 1.0, ALU.mult, ALU.add)
                    dv.inc(v.tensor_mul(row(7), row(7), row(5)))
                DV_G = dv.v
                @block.scalar
                def _(s):
                    s.wait_ge(sem_dv, DV_G)
                    s.copy(grow[:, 0:1024], row(6))
                    sc.inc(s.copy(grow[:, 1024:2048], row(7)))
                for e in range(NE):
                    SC_GR = sc.v
                    @block.tensor
                    def _(tt, e=e):
                        tt.wait_ge(sem_sc, SC_GR)
                        for th in range(2):
                            pc.mm(tt, (rpb0 if th == 0 else rpb1)[:], ones_row[:],
                                  grow[0:1, e * 1024 + th * 512:e * 1024 + (th + 1) * 512],
                                  start=True, stop=True, ncols=512)
                        pe.inc(tt.nop())
                    PE_GB = pe.v
                    @block.scalar
                    def _(s, e=e):
                        s.wait_ge(sem_pe, PE_GB)
                        s.copy(gbc[:, e, 0:512], rpb0[:])
                        sc.inc(s.copy(gbc[:, e, 512:1024], rpb1[:]))
            return sc.v

        def acc_ap(dc):
            return (F1a if dc < 8 else F1b)[:, dc % 8, :]

        def emit_ffn(xn_done, gates_done, final):
            with (
                nc.psum_tensor(uniq("fph0"), [128, 512], F32) as fph0,
                nc.psum_tensor(uniq("fph1"), [128, 512], F32) as fph1,
                nc.psum_tensor(uniq("fpo0"), [128, 512], F32) as fpo0,
                nc.psum_tensor(uniq("fpo1"), [128, 512], F32) as fpo1,
                nc.psum_tensor(uniq("fpo2"), [128, 512], F32) as fpo2,
                nc.psum_tensor(uniq("fpo3"), [128, 512], F32) as fpo3,
                nc.sbuf_tensor(uniq("fc_st"), [128, 2, 16, 128], BF16) as fc_st,
                nc.sbuf_tensor(uniq("pj_st"), [128, 2, GH, 128], BF16) as pj_st,
                nc.sbuf_tensor(uniq("sA"), [128, 2, 512], F32) as sA,
                nc.sbuf_tensor(uniq("sB"), [128, 2, 512], F32) as sB,
                nc.sbuf_tensor(uniq("sq_sb"), [128, 2, 512], BF16) as sq_sb,
                nc.sbuf_tensor(uniq("tmp_ev"), [128, 2, 512], F32) as tmp_ev,
                nc.sbuf_tensor(uniq("resbuf"), [128, 1 if final else 2, 1024],
                               F32 if final else BF16) as resbuf,
            ):
                xres = resbuf
                out_st = resbuf
                ps_h = [fph0, fph1]
                ps_o = [fpo0, fpo1, fpo2, fpo3]
                SC0, DV0 = sc.v, dv.v
                fc_loads, pj_loads = [], []
                fc_cons, pj_cons = [], []
                ab_done, gate_done, sqv = [], [], []
                oev, oacc = [], []
                proj_pc_done = None

                def fc_dma(k):
                    ge, sub = k // GH, k % GH
                    gi, e = ge // NE, ge % NE
                    @block.gpsimd
                    def _(g, k=k, gi=gi, e=e, sub=sub):
                        if k < 2:
                            g.wait_ge(sem_sc, SC0)
                            g.wait_ge(sem_dv, DV0)
                        else:
                            g.wait_ge(sem_pe, fc_cons[k - 2])
                        fc_loads.append(gl.inc(g.dma_start(
                            fc_st[:, k % 2, :, :],
                            d_fc[e, gi, sub].rearrange("d p c -> p d c")), 16))

                def pj_dma(k):
                    ge, ds = k // 16, k % 16
                    gi, e = ge // NE, ge % NE
                    @block.gpsimd
                    def _(g, k=k, gi=gi, e=e, ds=ds):
                        if k < 2:
                            g.wait_ge(sem_sc, SC0)
                            g.wait_ge(sem_dv, DV0)
                        else:
                            g.wait_ge(sem_pe, pj_cons[k - 2])
                        pj_loads.append(gl.inc(g.dma_start(
                            pj_st[:, k % 2, :, :],
                            d_pj[e, gi, ds].rearrange("h p c -> p h c")), 16))

                fc_dma(0); fc_dma(1)
                pj_dma(0); pj_dma(1)
                if upto == "fcw0":
                    @block.gpsimd
                    def _(g):
                        g.wait_ge(sem_gl, fc_loads[1])
                        so.inc(g.dma_start(
                            d_out.rearrange("p a b -> p (a b)")[:, 0:4096].rearrange(
                                "p (a b c) -> p a b c", a=2, b=16), fc_st[:]), 16)
                    @block.sync
                    def _(sy):
                        sy.wait_ge(sem_out, so.v)
                    return -2

                for ge in range(NGRP * NE):
                    gi, e = ge // NE, ge % NE
                    for sub in range(GH):
                        k = ge * GH + sub
                        for th in range(2):
                            hg = k * 2 + th
                            bank = ps_h[hg % 2]
                            @block.tensor
                            def _(tt, k=k, th=th, hg=hg, bank=bank):
                                if th == 0:
                                    tt.wait_ge(sem_gl, fc_loads[k])
                                    if k == 0:
                                        tt.wait_ge(sem_dv, xn_done)
                                        tt.wait_ge(sem_sc, gates_done)
                                if hg >= 2:
                                    tt.wait_ge(sem_sc, ab_done[hg - 2])
                                for dc in range(NCH):
                                    pc.mm(tt, bank[:], fc_st[:, k % 2, dc, :],
                                          B2[:, dc, th * 512:(th + 1) * 512],
                                          start=(dc == 0), stop=(dc == NCH - 1),
                                          ncols=512)
                                if th == 1:
                                    fc_cons.append(pe.inc(tt.nop()))
                            PC_H = pc.count
                            @block.scalar
                            def _(s, hg=hg, bank=bank):
                                s.wait_ge(sem_pace, PC_H)
                                if hg >= 2:
                                    s.wait_ge(sem_dv, gate_done[hg - 2])
                                s.activation(sA[:, hg % 2, :], bank[:],
                                             AF.Abs, scale=0.25)
                                ab_done.append(sc.inc(s.activation(
                                    sB[:, hg % 2, :], bank[:], AF.Copy, scale=0.75)))
                            @block.vector
                            def _(v, hg=hg):
                                v.wait_ge(sem_sc, ab_done[hg])
                                dv.inc(v.tensor_add(
                                    sA[:, hg % 2, :], sA[:, hg % 2, :], sB[:, hg % 2, :]))
                            DV_T3 = dv.v
                            @block.scalar
                            def _(s, hg=hg):
                                s.wait_ge(sem_dv, DV_T3)
                                sqv.append(sc.inc(s.activation(
                                    sq_sb[:, hg % 2, :], sA[:, hg % 2, :], AF.Square)))
                            @block.vector
                            def _(v, hg=hg, sub=sub, th=th, e=e, ge=ge):
                                v.wait_ge(sem_sc, sqv[hg])
                                if hg == 0:
                                    v.wait_ge(sem_sc, gates_done)
                                if ge > 0 and sub == 0 and th == 0:
                                    v.wait_ge(sem_pace, proj_pc_done)
                                gate_done.append(dv.inc(v.tensor_mul(
                                    B1[:, sub, th * 512:(th + 1) * 512],
                                    sq_sb[:, hg % 2, :],
                                    gbc[:, e, th * 512:(th + 1) * 512])))
                        if k + 2 < NGRP * NE * GH:
                            fc_dma(k + 2)
                    A_READY = dv.v
                    if upto == "a0" and ge == 0:
                        @block.gpsimd
                        def _(g):
                            g.wait_ge(sem_dv, A_READY)
                            so.inc(g.dma_start(d_out[:], B1[:]), 16)
                        @block.sync
                        def _(sy):
                            sy.wait_ge(sem_out, so.v)
                        return -1
                    for ds in range(16):
                        j = ge * 16 + ds
                        @block.tensor
                        def _(tt, j=j, ds=ds):
                            tt.wait_ge(sem_gl, pj_loads[j])
                            tt.wait_ge(sem_dv, A_READY)
                            if j >= 2:
                                tt.wait_ge(sem_sc, oev[j - 2])
                            for th in range(2):
                                bank = ps_o[(j % 2) * 2 + th]
                                for hs in range(GH):
                                    pc.mm(tt, bank[:], pj_st[:, j % 2, hs, :],
                                          B1[:, hs, th * 512:(th + 1) * 512],
                                          start=(hs == 0), stop=(hs == GH - 1),
                                          ncols=512)
                            pj_cons.append(pe.inc(tt.nop()))
                        PC_O = pc.count
                        @block.scalar
                        def _(s, j=j):
                            s.wait_ge(sem_pace, PC_O)
                            if j >= 1:
                                s.wait_ge(sem_dv, oacc[j - 1])
                            s.copy(tmp_ev[:, 0, :], ps_o[(j % 2) * 2][:])
                            oev.append(sc.inc(s.copy(
                                tmp_ev[:, 1, :], ps_o[(j % 2) * 2 + 1][:])))
                        @block.vector
                        def _(v, j=j, ds=ds, ge=ge):
                            v.wait_ge(sem_sc, oev[j])
                            if ge == 0:
                                v.tensor_copy(acc_ap(ds)[:, 0:512], tmp_ev[:, 0, :])
                                oacc.append(dv.inc(v.tensor_copy(
                                    acc_ap(ds)[:, 512:1024], tmp_ev[:, 1, :])))
                            else:
                                v.tensor_add(acc_ap(ds)[:, 0:512],
                                             acc_ap(ds)[:, 0:512], tmp_ev[:, 0, :])
                                oacc.append(dv.inc(v.tensor_add(
                                    acc_ap(ds)[:, 512:1024],
                                    acc_ap(ds)[:, 512:1024], tmp_ev[:, 1, :])))
                        if j + 2 < NGRP * NE * 16:
                            pj_dma(j + 2)
                    proj_pc_done = pc.count
                res_loads, res_used, out_dmas = [], [], []
                for dc in range(NCH):
                    if not final:
                        @block.gpsimd
                        def _(g, dc=dc):
                            if dc >= 2:
                                g.wait_ge(sem_dv, res_used[dc - 2])
                            res_loads.append(gl.inc(g.dma_start(
                                xres[:, dc % 2, :], d_x[:, dc, :]), 16))
                    @block.vector
                    def _(v, dc=dc):
                        v.wait_ge(sem_dv, oacc[(NGRP * NE - 1) * 16 + dc])
                        if final:
                            if dc >= 1:
                                v.wait_ge(sem_out, out_dmas[dc - 1])
                            src2 = B3[:, dc, :]
                            tgt = out_st[:, 0, :]
                        else:
                            v.wait_ge(sem_gl, res_loads[dc])
                            src2 = xres[:, dc % 2, :]
                            tgt = B3[:, dc, :]
                        res_used.append(dv.inc(v.scalar_tensor_tensor(
                            tgt, acc_ap(dc)[:], 0.5, src2, ALU.mult, ALU.add)))
                    @block.sync
                    def _(sy, dc=dc):
                        sy.wait_ge(sem_dv, res_used[dc])
                        if final:
                            out_dmas.append(so.inc(sy.dma_start(
                                d_out[:, dc, :], out_st[:, 0, :]), 16))
                        else:
                            sl.inc(sy.dma_start(d_x2[:, dc, :], B3[:, dc, :]), 16)
            return dv.v

        xn_done = emit_layernorm(B1, sem_gl, 16)
        if upto == "ln1":
            dump_and_stop(B2, sem_dv, xn_done)
            es.close()
            return nc
        gates_done = emit_router(xn_done)
        if upto == "gates":
            @block.gpsimd
            def _(g):
                g.wait_ge(sem_sc, gates_done)
                so.inc(g.dma_start(d_out[:, 0:2, :], gbc[:]), 16)
            @block.sync
            def _(sy):
                sy.wait_ge(sem_out, so.v)
            es.close()
            return nc
        x2_done = emit_ffn(xn_done, gates_done, final=False)
        if upto == "fcw0":
            es.close()
            return nc
        if upto == "a0":
            es.close()
            return nc
        if upto == "acc":
            @block.gpsimd
            def _(g):
                g.wait_ge(sem_dv, x2_done)
                so.inc(g.dma_start(d_out[:, 0:8, :], F1a[:]), 16)
                so.inc(g.dma_start(d_out[:, 8:16, :], F1b[:]), 16)
            @block.sync
            def _(sy):
                sy.wait_ge(sem_out, so.v)
            es.close()
            return nc
        if upto == "ffn1":
            dump_and_stop(B3, sem_dv, x2_done)
            es.close()
            return nc
        fence()

        xn2_done = emit_layernorm(B3, sem_dv, x2_done)
        if upto == "ln2":
            dump_and_stop(B2, sem_dv, xn2_done)
            es.close()
            return nc

        with (
            nc.psum_tensor(uniq("qkp0"), [128, 512], F32) as qkp0,
            nc.psum_tensor(uniq("qkp1"), [128, 512], F32) as qkp1,
            nc.sbuf_tensor(uniq("w_st"), [128, 2, 8, 512], BF16) as w_st,
            nc.sbuf_tensor(uniq("qkv_stage"), [128, 4, 1024], BF16) as qkv_stage,
        ):
            wk_loads, wk_cons = [], []
            qk_ev, stage_dma = [], []
            wsrc = [d_wq, d_wk, d_wv]
            SCq, DVq = sc.v, dv.v

            def w_dma(kk):
                t, half = kk // 2, kk % 2
                @block.gpsimd
                def _(g, kk=kk, t=t, half=half):
                    if kk < 2:
                        g.wait_ge(sem_sc, SCq)
                        g.wait_ge(sem_dv, DVq)
                    else:
                        g.wait_ge(sem_pe, wk_cons[(kk - 2) // 2])
                    wk_loads.append(gl.inc(g.dma_start(
                        w_st[:, kk % 2, :, :],
                        wsrc[t][:, half * 8:(half + 1) * 8, :]), 16))
            w_dma(0); w_dma(1)

            for t in range(3):
                for rs in range(4):
                    for th in range(2):
                        gq = (t * 4 + rs) * 2 + th
                        bank = qkp0 if gq % 2 == 0 else qkp1
                        @block.tensor
                        def _(tt, t=t, rs=rs, th=th, gq=gq, bank=bank):
                            if gq == 0:
                                tt.wait_ge(sem_dv, xn2_done)
                            if rs == 0 and th == 0:
                                tt.wait_ge(sem_gl, wk_loads[t * 2 + 1])
                            if gq >= 2:
                                tt.wait_ge(sem_sc, qk_ev[gq - 2])
                            for dc in range(NCH):
                                kk = t * 2 + dc // 8
                                pc.mm(tt, bank[:],
                                      w_st[:, kk % 2, dc % 8, rs * 128:(rs + 1) * 128],
                                      B2[:, dc, th * 512:(th + 1) * 512],
                                      start=(dc == 0), stop=(dc == NCH - 1), ncols=512)
                            if rs == 3 and th == 1:
                                wk_cons.append(pe.inc(tt.nop()))
                        PC_Q = pc.count
                        @block.scalar
                        def _(s, t=t, rs=rs, th=th, gq=gq, bank=bank):
                            s.wait_ge(sem_pace, PC_Q)
                            if t >= 1 and rs == 0 and th == 0:
                                s.wait_ge(sem_sl, stage_dma[t - 1])
                            qk_ev.append(sc.inc(s.copy(
                                qkv_stage[:, rs, th * 512:(th + 1) * 512], bank[:])))
                @block.sync
                def _(sy, t=t):
                    sy.wait_ge(sem_sc, qk_ev[-1])
                    stage_dma.append(sl.inc(sy.dma_start(
                        d_qkv_in[t * R:(t + 1) * R, :].rearrange("(a p) c -> p a c", p=128),
                        qkv_stage[:]), 16))
                if t < 2:
                    w_dma(t * 2 + 2); w_dma(t * 2 + 3)

            @block.gpsimd
            def _(g):
                g.wait_ge(sem_sl, stage_dma[-1])
                cc.inc(g.collective_compute(
                    "AllGather", ALU.bypass, replica_groups=rg,
                    ins=[d_qkv_in.opt()], outs=[d_qkv_out.opt()]), 1)
        CC_QKV = cc.v
        if upto == "qkv":
            @block.gpsimd
            def _(g):
                g.wait_ge(sem_cc, CC_QKV)
                so.inc(g.dma_start(
                    d_out.rearrange("p a b -> a p b")[0:12],
                    d_qkv_out[0:1536, :].rearrange("(q p) c -> q p c", p=128)), 16)
            @block.sync
            def _(sy):
                sy.wait_ge(sem_out, so.v)
            es.close()
            return nc
        fence()

        with (
            nc.psum_tensor(uniq("vps0"), [128, 128], F32) as vps0,
            nc.psum_tensor(uniq("vps1"), [128, 128], F32) as vps1,
            nc.sbuf_tensor(uniq("vd_st"), [128, 2, 4, 1024], BF16) as vd_st,
            nc.sbuf_tensor(uniq("vu_out"), [128, 2, 8, 128], BF16) as vu_out,
        ):
            vd_loads, vd_cons = [], []
            vu_ev, vu_dma = [], []

            def vd_dma(r):
                @block.sync
                def _(sy, r=r):
                    if r == 0:
                        sy.wait_ge(sem_cc, CC_QKV)
                    if r >= 2:
                        sy.wait_ge(sem_pe, vd_cons[r - 2])
                    src = d_qkv_out[r * 1536 + 1024: r * 1536 + 1536, :]
                    vd_loads.append(sl.inc(sy.dma_start(
                        vd_st[:, r % 2, :, :],
                        src.rearrange("(a p) c -> p a c", p=128)), 16))
            vd_dma(0); vd_dma(1)

            for r in range(NCORES):
                for h in range(2):
                    for ss in range(8):
                        gv = (r * 2 + h) * 8 + ss
                        bank = vps0 if gv % 2 == 0 else vps1
                        @block.tensor
                        def _(tt, r=r, h=h, ss=ss, gv=gv, bank=bank):
                            if h == 0 and ss == 0:
                                tt.wait_ge(sem_sl, vd_loads[r])
                                tt.wait_ge(sem_gl, GL_P0)
                            if gv >= 2:
                                tt.wait_ge(sem_sc, vu_ev[gv - 2])
                            for rc in range(4):
                                pc.mm(tt, bank[:],
                                      vd_st[:, r % 2, rc, ss * 128:(ss + 1) * 128],
                                      vupT_sb[:, h, rc, :],
                                      start=(rc == 0), stop=(rc == 3), ncols=128)
                            if h == 1 and ss == 7:
                                vd_cons.append(pe.inc(tt.nop()))
                        PC_V = pc.count
                        @block.scalar
                        def _(s, r=r, h=h, ss=ss, gv=gv, bank=bank):
                            s.wait_ge(sem_pace, PC_V)
                            if r * 2 + h >= 2:
                                s.wait_ge(sem_sl, vu_dma[r * 2 + h - 2])
                            vu_ev.append(sc.inc(s.copy(
                                vu_out[:, h % 2, ss, :], bank[:])))
                    @block.sync
                    def _(sy, r=r, h=h):
                        sy.wait_ge(sem_sc, vu_ev[-1])
                        vu_dma.append(sl.inc(sy.dma_start(
                            d_vup[h, r * 8:(r + 1) * 8].rearrange("s p c -> p s c"),
                            vu_out[:, h % 2, :, :]), 16))
                if r + 2 < NCORES:
                    vd_dma(r + 2)
        fence()

        with (
            nc.psum_tensor(uniq("alg0"), [128, 512], F32) as alg0,
            nc.psum_tensor(uniq("alg1"), [128, 512], F32) as alg1,
            nc.psum_tensor(uniq("aqm0"), [128, 512], F32) as aqm0,
            nc.psum_tensor(uniq("aqm1"), [128, 512], F32) as aqm1,
            nc.psum_tensor(uniq("apt0"), [128, 1024], BF16) as apt0,
            nc.psum_tensor(uniq("apt1"), [128, 1024], BF16) as apt1,
            nc.psum_tensor(uniq("aav0"), [128, 512], F32) as aav0,
            nc.psum_tensor(uniq("aav1"), [128, 512], F32) as aav1,
            nc.sbuf_tensor(uniq("mask_sb"), [128, 4, 512], F32) as mask_sb,
            nc.sbuf_tensor(uniq("qd_st"), [128, 2, 4, 512], BF16) as qd_st,
            nc.sbuf_tensor(uniq("m_sb"), [128, 4, 512], BF16) as m_sb,
            nc.sbuf_tensor(uniq("qm_sb"), [128, 4, 512], BF16) as qm_sb,
            nc.sbuf_tensor(uniq("vup_st"), [128, 2, 16, 128], BF16) as vup_st,
            nc.sbuf_tensor(uniq("pt_sb"), [128, 2, 256], BF16) as pt_sb,
            nc.sbuf_tensor(uniq("ctx_st"), [128, 2, 256], BF16) as ctx_st,
        ):
            ps_lg = [alg0, alg1]
            ps_qm = [aqm0, aqm1]
            ps_pt = [apt0, apt1]
            ps_av = [aav0, aav1]

            @block.gpsimd
            def _(g):
                gl.inc(g.dma_start(mask_sb[:], d_mask), 16)
            GL_MASK = gl.v

            def kd_slice(rc, lo, hi):
                f = B1f if rc < 2 else B2f
                return f[:, (rc % 2) * T + lo:(rc % 2) * T + hi]

            def p2_slice(slot, lo, hi):
                return B3f[:, slot * T + lo:slot * T + hi]

            def lg_slice(lo, hi):
                return F1af[:, lo:hi]

            kd_loads = []
            for r in range(NCORES):
                @block.sync
                def _(sy, r=r):
                    if r == 0:
                        sy.wait_ge(sem_cc, CC_QKV)
                    src = d_qkv_out[r * 1536 + 512: r * 1536 + 1024, :]
                    for rc in range(4):
                        ins = sy.dma_start(
                            kd_slice(rc, r * TOK, (r + 1) * TOK),
                            src[rc * 128:(rc + 1) * 128, :])
                        sl.inc(ins, 16)
                kd_loads.append(sl.v)
            KD_ALL = sl.v

            qd_loads = [None] * (2 * NQC)
            qd_cons = [None] * (2 * NQC)
            qm_bank_last = [None, None]
            lg_bank_last = [None, None]
            last_av_pc = None
            last_ctx_for_grp = [None, None]
            vup_load_vals, vup_cons_vals = [], []
            vgen = 0
            ptg = 0
            pt_ev = {}
            av_pace_after = {}
            ctx_dma = []
            m_ev = [[], []]

            def qd_dma(idx):
                @block.sync
                def _(sy, idx=idx):
                    if idx == 0:
                        sy.wait_ge(sem_cc, CC_QKV)
                    if idx >= 2:
                        sy.wait_ge(sem_pace, qd_cons[idx - 2])
                    qc = idx % NQC
                    rr, half = qc // 2, qc % 2
                    src = d_qkv_out[rr * 1536: rr * 1536 + 512,
                                    half * 512:(half + 1) * 512]
                    qd_loads[idx] = sl.inc(sy.dma_start(
                        qd_st[:, idx % 2, :, :],
                        src.rearrange("(a p) c -> p a c", p=128)), 16)
            qd_dma(0); qd_dma(1)

            stop_flag = []
            for h in range(2):
                if stop_flag:
                    break
                for qs in range(4):
                    bank = ps_qm[qs % 2]
                    @block.tensor
                    def _(tt, h=h, qs=qs, bank=bank):
                        if h == 0 and qs == 0:
                            tt.wait_ge(sem_gl, GL_P0)
                        prev = qm_bank_last[qs % 2]
                        if prev is not None:
                            tt.wait_ge(sem_sc, prev)
                        pc.mm(tt, bank[:], qup_sb[:, h, qs * 128:(qs + 1) * 128],
                              kup_sb[:, h, :], start=True, stop=True, ncols=512)
                        pe.inc(tt.nop())
                    PE_M = pe.v
                    @block.scalar
                    def _(s, h=h, qs=qs, bank=bank):
                        s.wait_ge(sem_pe, PE_M)
                        m_ev[h].append(sc.inc(s.activation(
                            m_sb[:, qs, :], bank[:], AF.Copy, scale=RSCALE)))
                    qm_bank_last[qs % 2] = sc.v

                for qc in range(NQC):
                    if stop_flag:
                        break
                    idx = h * NQC + qc
                    nk = qc + 1
                    for ks in range(4):
                        bank = ps_qm[ks % 2]
                        @block.tensor
                        def _(tt, h=h, idx=idx, ks=ks, bank=bank):
                            if ks == 0:
                                tt.wait_ge(sem_sl, qd_loads[idx])
                                tt.wait_ge(sem_sc, m_ev[h][3])
                            prev = qm_bank_last[ks % 2]
                            if prev is not None:
                                tt.wait_ge(sem_sc, prev)
                            for qs in range(4):
                                pc.mm(tt, bank[:],
                                      m_sb[:, qs, ks * 128:(ks + 1) * 128],
                                      qd_st[:, idx % 2, qs, :],
                                      start=(qs == 0), stop=(qs == 3), ncols=512)
                        PC_QM = pc.count
                        @block.scalar
                        def _(s, ks=ks, bank=bank):
                            s.wait_ge(sem_pace, PC_QM)
                            sc.inc(s.copy(qm_sb[:, ks, :], bank[:]))
                        qm_bank_last[ks % 2] = sc.v
                    QM_DONE = sc.v
                    qd_cons[idx] = pc.count
                    if idx + 2 < 2 * NQC:
                        qd_dma(idx + 2)

                    for grp in range(2):
                        p2_ready = []
                        for qtl in range(2):
                            qt = grp * 2 + qtl
                            lg_evs = []
                            for kc in range(nk):
                                bank = ps_lg[kc % 2]
                                @block.tensor
                                def _(tt, qt=qt, kc=kc, bank=bank):
                                    if kc == 0:
                                        tt.wait_ge(sem_sc, QM_DONE)
                                        tt.wait_ge(sem_sl, KD_ALL)
                                    prev = lg_bank_last[kc % 2]
                                    if prev is not None:
                                        tt.wait_ge(sem_sc, prev)
                                    for ks in range(4):
                                        pc.mm(tt, bank[:],
                                              qm_sb[:, ks, qt * 128:(qt + 1) * 128],
                                              kd_slice(ks, kc * 512, (kc + 1) * 512),
                                              start=(ks == 0), stop=(ks == 3),
                                              ncols=512)
                                PC_LG = pc.count
                                @block.scalar
                                def _(s, kc=kc, bank=bank):
                                    s.wait_ge(sem_pace, PC_LG)
                                    lg_evs.append(sc.inc(s.copy(
                                        lg_slice(kc * 512, (kc + 1) * 512), bank[:])))
                                lg_bank_last[kc % 2] = sc.v
                            LGE = lg_evs[-1]
                            @block.vector
                            def _(v, qt=qt, nk=nk, LGE=LGE):
                                v.wait_ge(sem_sc, LGE)
                                v.wait_ge(sem_gl, GL_MASK)
                                v.tensor_add(lg_slice((nk - 1) * 512, nk * 512),
                                             lg_slice((nk - 1) * 512, nk * 512),
                                             mask_sb[:, qt, :])
                                dv.inc(v.tensor_reduce(
                                    colstat[:, qt:qt + 1], lg_slice(0, nk * 512),
                                    AX.X, ALU.max, negate=True))
                            DV_MX = dv.v
                            @block.scalar
                            def _(s, qt=qt, qtl=qtl, nk=nk, DV_MX=DV_MX,
                                  lap=last_av_pc):
                                s.wait_ge(sem_dv, DV_MX)
                                if lap is not None:
                                    s.wait_ge(sem_pace, lap)
                                sc.inc(s.activation(
                                    p2_slice(qtl, 0, nk * 512), lg_slice(0, nk * 512),
                                    AF.Exp, bias=colstat[:, qt:qt + 1],
                                    accum_out=colstat[:, 4 + qt:5 + qt]))
                            SC_EXP = sc.v
                            @block.vector
                            def _(v, qt=qt, SC_EXP=SC_EXP):
                                v.wait_ge(sem_sc, SC_EXP)
                                dv.inc(v.reciprocal(colstat[:, 8 + qt:9 + qt],
                                                    colstat[:, 4 + qt:5 + qt]))
                        @block.vector
                        def _(v, grp=grp, nk=nk):
                            for qtl in range(2):
                                qt = grp * 2 + qtl
                                dv.inc(v.tensor_scalar(
                                    p2_slice(qtl, 0, nk * 512),
                                    p2_slice(qtl, 0, nk * 512),
                                    colstat[:, 8 + qt:9 + qt], None, ALU.mult))
                        p2_ready = [dv.v - 1, dv.v]
                        if upto == "p2" and h == 0 and qc == 2 and grp == 0:
                            @block.gpsimd
                            def _(g):
                                g.wait_ge(sem_dv, p2_ready[1])
                                so.inc(g.dma_start(d_out[:, 0:8, :],
                                                   B3f[:, 0:8192].rearrange(
                                                       "p (a b) -> p a b", a=8)), 16)
                                so.inc(g.dma_start(d_out[:, 8:16, :],
                                                   B3f[:, 8192:16384].rearrange(
                                                       "p (a b) -> p a b", a=8)), 16)
                                so.inc(g.dma_start(d_out[:, 15, 0:16], colstat[:]), 16)
                            @block.sync
                            def _(sy):
                                sy.wait_ge(sem_out, so.v)
                            stop_flag.append(1)
                        if stop_flag:
                            break
                        nvl = (nk + 3) // 4
                        first_vl = vgen

                        def vup_dma(h_, vl_):
                            vg_ = first_vl + vl_
                            @block.sync
                            def _(sy, h_=h_, vl_=vl_, vg_=vg_):
                                if vg_ >= 2:
                                    sy.wait_ge(sem_pace, vup_cons_vals[vg_ - 2])
                                vup_load_vals.append(sl.inc(sy.dma_start(
                                    vup_st[:, vg_ % 2, :, :],
                                    d_vup[h_, vl_ * 16: vl_ * 16 + 16].rearrange(
                                        "s p c -> p s c")), 16))
                            vup_cons_vals.append(None)

                        vup_dma(h, 0)
                        if nvl > 1:
                            vup_dma(h, 1)
                        vgen += nvl
                        for kc in range(nk):
                            vg = first_vl + kc // 4
                            for ssub in range(4):
                                pgen = ptg
                                ptg += 1
                                bank_pt = ps_pt[pgen % 2]
                                @block.tensor
                                def _(tt, grp=grp, kc=kc, ssub=ssub, pgen=pgen,
                                      bank_pt=bank_pt, vg=vg,
                                      p2r=p2_ready, lc=last_ctx_for_grp[grp]):
                                    if kc == 0 and ssub == 0:
                                        tt.wait_ge(sem_dv, p2r[1])
                                        if lc is not None:
                                            tt.wait_ge(sem_sc, lc)
                                    if ssub == 0 and kc % 4 == 0:
                                        tt.wait_ge(sem_sl, vup_load_vals[vg])
                                    if pgen >= 2:
                                        tt.wait_ge(sem_dv, pt_ev[pgen - 2])
                                    for qtl in range(2):
                                        tt.transpose(
                                            bank_pt[:, qtl * 128:(qtl + 1) * 128],
                                            p2_slice(qtl, kc * 512 + ssub * 128,
                                                     kc * 512 + ssub * 128 + 128),
                                            id_sb[:])
                                        tt.nop()
                                    pe.inc(tt.nop())
                                PE_PT = pe.v
                                @block.vector
                                def _(v, pgen=pgen, bank_pt=bank_pt, PE_PT=PE_PT):
                                    v.wait_ge(sem_pe, PE_PT)
                                    if pgen >= 2 and (pgen - 2) in av_pace_after:
                                        v.wait_ge(sem_pace, av_pace_after[pgen - 2])
                                    pt_ev[pgen] = dv.inc(v.tensor_copy(
                                        pt_sb[:, pgen % 2, :], bank_pt[:, 0:256]))
                                @block.tensor
                                def _(tt, grp=grp, kc=kc, ssub=ssub, pgen=pgen,
                                      vg=vg, nk=nk):
                                    tt.wait_ge(sem_dv, pt_ev[pgen])
                                    pc.mm(tt, ps_av[grp][:, 0:256],
                                          vup_st[:, vg % 2, (kc % 4) * 4 + ssub, :],
                                          pt_sb[:, pgen % 2, :],
                                          start=(kc == 0 and ssub == 0),
                                          stop=(kc == nk - 1 and ssub == 3),
                                          ncols=256)
                                av_pace_after[pgen] = pc.count
                            if kc % 4 == 3 or kc == nk - 1:
                                vup_cons_vals[first_vl + kc // 4] = pc.count
                                nxt = kc // 4 + 2
                                if nxt < nvl:
                                    vup_dma(h, nxt)
                        PC_AV = pc.count
                        last_av_pc = PC_AV
                        @block.scalar
                        def _(s, grp=grp, PC_AV=PC_AV,
                              nd=len(ctx_dma)):
                            s.wait_ge(sem_pace, PC_AV)
                            if nd >= 2:
                                s.wait_ge(sem_sl, ctx_dma[nd - 2])
                            sc.inc(s.copy(ctx_st[:, grp, :], ps_av[grp][:, 0:256]))
                        last_ctx_for_grp[grp] = sc.v
                        CTXEV = sc.v
                        @block.sync
                        def _(sy, h=h, qc=qc, grp=grp, CTXEV=CTXEV):
                            sy.wait_ge(sem_sc, CTXEV)
                            jj, half = qc // 2, qc % 2
                            base = half * 512 + grp * 256
                            ctx_dma.append(sl.inc(sy.dma_start(
                                d_ctx_in[jj * 256 + h * 128: jj * 256 + (h + 1) * 128,
                                         base: base + 256],
                                ctx_st[:, grp, :]), 16))
            CTX_ALL = sl.v
            if not stop_flag:
                @block.gpsimd
                def _(g):
                    g.wait_ge(sem_sl, CTX_ALL)
                    with allow_ici_all_to_all():
                        cc.inc(g.collective_compute(
                            "AllToAll", ALU.bypass, replica_groups=rg,
                            ins=[d_ctx_in.opt()], outs=[d_ctx_out.opt()]), 1)
        if stop_flag:
            es.close()
            return nc
        CC_CTX = cc.v
        if upto == "attn":
            @block.gpsimd
            def _(g):
                g.wait_ge(sem_cc, CC_CTX)
                so.inc(g.dma_start(
                    d_out[:, 0:16, :], d_ctx_out.rearrange("(a p) c -> p a c", p=128)), 16)
            @block.sync
            def _(sy):
                sy.wait_ge(sem_out, so.v)
            es.close()
            return nc
        fence()

        with (
            nc.psum_tensor(uniq("op0"), [128, 512], F32) as op0,
            nc.psum_tensor(uniq("op1"), [128, 512], F32) as op1,
            nc.psum_tensor(uniq("op2"), [128, 512], F32) as op2,
            nc.psum_tensor(uniq("op3"), [128, 512], F32) as op3,
            nc.sbuf_tensor(uniq("wo_st"), [128, 2, 16, 128], BF16) as wo_st,
            nc.sbuf_tensor(uniq("otmp"), [128, 2, 512], F32) as otmp,
            nc.sbuf_tensor(uniq("x2res"), [128, 2, 1024], BF16) as x2res,
        ):
            ps_op = [op0, op1, op2, op3]
            @block.sync
            def _(sy):
                sy.wait_ge(sem_cc, CC_CTX)
                sl.inc(sy.dma_start(
                    B2[:], d_ctx_out.rearrange("(a p) c -> p a c", p=128)), 16)
            CTX_OWN = sl.v

            wo_loads, wo_cons = [], []
            oev2, ores2 = [], []
            x2_loads, x2_used = [], []

            def wo_dma(ds):
                @block.gpsimd
                def _(g, ds=ds):
                    if ds >= 2:
                        g.wait_ge(sem_pe, wo_cons[ds - 2])
                    wo_loads.append(gl.inc(g.dma_start(
                        wo_st[:, ds % 2, :, :],
                        d_wo[ds].rearrange("d p c -> p d c")), 16))

            def x2_dma(dc):
                @block.gpsimd
                def _(g, dc=dc):
                    if dc >= 2:
                        g.wait_ge(sem_dv, x2_used[dc - 2])
                    x2_loads.append(gl.inc(
                        g.dma_start(x2res[:, dc % 2, :], d_x2[:, dc, :]), 16))

            wo_dma(0); wo_dma(1)
            x2_dma(0); x2_dma(1)

            for ds in range(16):
                @block.tensor
                def _(tt, ds=ds):
                    tt.wait_ge(sem_gl, wo_loads[ds])
                    if ds == 0:
                        tt.wait_ge(sem_sl, CTX_OWN)
                    if ds >= 2:
                        tt.wait_ge(sem_sc, oev2[ds - 2])
                    for th in range(2):
                        bank = ps_op[(ds % 2) * 2 + th]
                        for dc in range(NCH):
                            pc.mm(tt, bank[:], wo_st[:, ds % 2, dc, :],
                                  B2[:, dc, th * 512:(th + 1) * 512],
                                  start=(dc == 0), stop=(dc == NCH - 1), ncols=512)
                    wo_cons.append(pe.inc(tt.nop()))
                PC_OP = pc.count
                @block.scalar
                def _(s, ds=ds):
                    s.wait_ge(sem_pace, PC_OP)
                    if ds >= 1:
                        s.wait_ge(sem_dv, ores2[ds - 1])
                    s.add(otmp[:, 0, :], ps_op[(ds % 2) * 2][:],
                          ob_sb[:, ds:ds + 1])
                    oev2.append(sc.inc(s.add(
                        otmp[:, 1, :], ps_op[(ds % 2) * 2 + 1][:],
                        ob_sb[:, ds:ds + 1])))
                @block.vector
                def _(v, ds=ds):
                    v.wait_ge(sem_sc, oev2[ds])
                    v.wait_ge(sem_gl, x2_loads[ds])
                    v.tensor_add(B3[:, ds, 0:512], otmp[:, 0, :],
                                 x2res[:, ds % 2, 0:512])
                    x2_used.append(dv.inc(v.tensor_add(
                        B3[:, ds, 512:1024], otmp[:, 1, :],
                        x2res[:, ds % 2, 512:1024])))
                ores2.append(dv.v)
                if ds + 2 < 16:
                    wo_dma(ds + 2)
                    x2_dma(ds + 2)
            X3_DONE = dv.v
        if upto == "outproj":
            dump_and_stop(B3, sem_dv, X3_DONE)
            es.close()
            return nc
        fence()

        xn3_done = emit_layernorm(B3, sem_dv, X3_DONE)
        gates3 = emit_router(xn3_done)
        emit_ffn(xn3_done, gates3, final=True)

        @block.sync
        def _(sy):
            sy.wait_ge(sem_out, so.v)

    es.close()
    return nc



_CACHE = {}
H = 16


def _prep(host):
    (x, qd, kd, vd, qu, ku, vu, ow, ob, wr, wfc, wpj) = host
    wfcT = np.ascontiguousarray(np.transpose(wfc, (0, 2, 1)))
    wpjT = np.ascontiguousarray(np.transpose(wpj, (0, 2, 1)))
    fc_s = np.ascontiguousarray(
        wfcT.reshape(NE, 16, 128, NGRP, GH, 128).transpose(0, 3, 4, 1, 2, 5))
    pj_s = np.ascontiguousarray(
        wpjT.reshape(NE, NGRP, GH, 128, 16, 128).transpose(0, 1, 4, 2, 3, 5))
    wo_s = np.ascontiguousarray(
        ow.T.reshape(16, 128, 16, 128).transpose(2, 0, 1, 3))
    wr_p = np.ascontiguousarray(wr.T.reshape(16, 128, NE).transpose(1, 0, 2))
    wq_p = np.ascontiguousarray(qd.T.reshape(16, 128, R).transpose(1, 0, 2))
    wk_p = np.ascontiguousarray(kd.T.reshape(16, 128, R).transpose(1, 0, 2))
    wv_p = np.ascontiguousarray(vd.T.reshape(16, 128, R).transpose(1, 0, 2))
    ob_p = np.ascontiguousarray(ob.reshape(16, 128).T)
    mask = np.zeros((4, 128, 512), np.float32)
    for qt in range(4):
        i = qt * 128 + np.arange(128)[:, None]
        j = np.arange(512)[None, :]
        mask[qt] = np.where(j <= i, 0.0, MASK_NEG)
    mask_p = np.ascontiguousarray(mask.transpose(1, 0, 2))
    ident = np.eye(128, dtype=np.float32).astype(ml_dtypes.bfloat16)
    qu_h = np.asarray(qu).reshape(H, 128, R)
    ku_h = np.asarray(ku).reshape(H, 128, R)
    vu_hT = np.ascontiguousarray(np.asarray(vu).reshape(H, 128, R).transpose(0, 2, 1))

    shared = dict(fc_s=fc_s, pj_s=pj_s, woT_s=wo_s, wr=wr_p,
                  wqdT=wq_p, wkdT=wk_p, wvdT=wv_p, outb=ob_p,
                  mask=mask_p, ident=ident)
    in_maps = []
    for c in range(NCORES):
        m = dict(shared)
        xs = np.asarray(x)[0, c * TOK:(c + 1) * TOK, :]
        m["x_fm"] = np.ascontiguousarray(
            xs.T.reshape(16, 128, TOK).transpose(1, 0, 2))
        m["qup"] = np.ascontiguousarray(qu_h[2 * c:2 * c + 2].transpose(1, 0, 2))
        m["kup"] = np.ascontiguousarray(ku_h[2 * c:2 * c + 2].transpose(1, 0, 2))
        m["vupT"] = np.ascontiguousarray(
            vu_hT[2 * c:2 * c + 2].reshape(2, 4, 128, 128).transpose(2, 0, 1, 3))
        in_maps.append(m)
    return in_maps


def kernel(L0_attn_x, L0_attn_q_down_w, L0_attn_k_down_w, L0_attn_v_down_w,
           L0_attn_q_up_w, L0_attn_k_up_w, L0_attn_v_up_w, L0_attn_out_w,
           L0_attn_out_b, L0_ffn_w_router, L0_ffn_w_fc, L0_ffn_w_proj):
    host = tuple(np.asarray(a) for a in (
        L0_attn_x, L0_attn_q_down_w, L0_attn_k_down_w, L0_attn_v_down_w,
        L0_attn_q_up_w, L0_attn_k_up_w, L0_attn_v_up_w, L0_attn_out_w,
        L0_attn_out_b, L0_ffn_w_router, L0_ffn_w_fc, L0_ffn_w_proj))
    in_maps = _prep(host)
    if "nc" not in _CACHE:
        _CACHE["nc"] = build_nc()
    res = run_bass_kernel_spmd(_CACHE["nc"], in_maps, core_ids=list(range(NCORES)))
    out = np.empty((1, T, D), np.float32)
    for c in range(NCORES):
        yc = res.results[c]["y"]
        out[0, c * TOK:(c + 1) * TOK, :] = yc.transpose(1, 0, 2).reshape(D, TOK).T
    return out
